# version 04
# baseline (speedup 1.0000x reference)
import numpy as np
import ml_dtypes
import concourse.bass as bass
import concourse.mybir as mybir
from concourse.bass_utils import run_bass_kernel_spmd
from contextlib import ExitStack

F32 = mybir.dt.float32
BF16 = mybir.dt.bfloat16
I16 = mybir.dt.int16
AF = mybir.ActivationFunctionType
ALU = mybir.AluOpType

NF4 = [-1.0, -0.6961928009986877, -0.5250730514526367, -0.39491748809814453,
       -0.28444138169288635, -0.18477343022823334, -0.09105003625154495, 0.0,
       0.07958029955625534, 0.16093020141124725, 0.24611230194568634, 0.33791524171829224,
       0.44070982933044434, 0.5626170039176941, 0.7229568362236023, 1.0]

H = 4096
M = 14336
B = 4
CSEQ = 2048
NCORE = 8
TLOC = 1024
CLOC = 256
NCH = 32
SBS = [8, 8, 16, 16, 16, 16, 16, 16]
SBOFF = [0, 8, 16, 32, 48, 64, 80, 96]
NSB = len(SBS)
DQW = 1024
EPS = 1e-5
SM_SCALE = 1.0 / float(np.sqrt(np.float32(CSEQ)))
PAIRS = [(0, 0), (0, 1), (0, 2), (0, 3), (1, 1), (1, 2), (1, 3), (2, 2), (2, 3), (3, 3)]
PIDX = {}
for _i, (_a, _b2) in enumerate(PAIRS):
    PIDX[(_a, _b2)] = _i
    PIDX[(_b2, _a)] = _i


def build_program():
    nc = bass.Bass("TRN2", target_bir_lowering=False)

    xT = nc.dram_tensor("xT", [H, TLOC], F32, kind="ExternalInput").ap()
    gq = nc.dram_tensor("gq", [512, M], I16, kind="ExternalInput").ap()
    uq = nc.dram_tensor("uq", [512, M], I16, kind="ExternalInput").ap()
    gam = nc.dram_tensor("gam", [8, M], F32, kind="ExternalInput").ap()
    uam = nc.dram_tensor("uam", [8, M], F32, kind="ExternalInput").ap()
    dq = nc.dram_tensor("dq", [14 * 128, H], I16, kind="ExternalInput").ap()
    dam = nc.dram_tensor("dam", [14 * 2, H], F32, kind="ExternalInput").ap()
    outT = nc.dram_tensor("outT", [H, TLOC], F32, kind="ExternalOutput").ap()

    x1h = nc.dram_tensor("x1h", [H, TLOC], F32, kind="Internal").ap()
    parth = nc.dram_tensor("parth", [H, TLOC], F32, kind="Internal").ap()
    sar_i = nc.dram_tensor("sar_i", [128, 320], F32, kind="Internal")
    sar_o = nc.dram_tensor("sar_o", [128, 320], F32, kind="Internal")
    ag_g_i, ag_g_o, ag_u_i, ag_u_o, ag_d_i, ag_d_o = [], [], [], [], [], []
    for s, sb in enumerate(SBS):
        msb = sb * 128
        ag_g_i.append(nc.dram_tensor(f"aggi{s}", [sb * 128, 512], BF16, kind="Internal"))
        ag_g_o.append(nc.dram_tensor(f"aggo{s}", [NCORE * sb * 128, 512], BF16, kind="Internal"))
        ag_u_i.append(nc.dram_tensor(f"agui{s}", [sb * 128, 512], BF16, kind="Internal"))
        ag_u_o.append(nc.dram_tensor(f"aguo{s}", [NCORE * sb * 128, 512], BF16, kind="Internal"))
        ag_d_i.append(nc.dram_tensor(f"agdi{s}", [(sb // 8) * 128, H], BF16, kind="Internal"))
        ag_d_o.append(nc.dram_tensor(f"agdo{s}", [sb * 128, H], BF16, kind="Internal"))

    st = ExitStack()
    T = lambda *a, **k: st.enter_context(nc.sbuf_tensor(*a, **k))
    SEM = lambda n: st.enter_context(nc.semaphore(n))

    xbT = T("xbT", [128, NCH, 1024], BF16)
    dqq = [T(f"dqq{i}", [128, DQW], I16) for i in range(2)]
    dqa = [T(f"dqa{i}", [128, DQW], F32) for i in range(2)]
    dqc = T("dqc", [128, DQW], BF16)
    dqt = T("dqt", [128, DQW], BF16)
    dqw = [T(f"dqw{i}", [128, DQW], BF16) for i in range(2)]
    ssb = T("ssb", [128, 320], F32)
    s16 = T("s16", [128, 512], F32)
    p16 = T("p16", [128, 512], F32)
    aux4 = T("aux4", [128, 128], F32)
    zd = T("zd", [128, 256], F32)
    rb = T("rb", [128, 1024], F32)
    mrb = T("mrb", [128, 1024], F32)
    rowA = T("rowA", [1, 1024], F32)
    rowB = T("rowB", [1, 1024], F32)
    onesf = T("onesf", [1, 128], F32)
    ones = T("ones", [128, 1], BF16)

    sxb = [SEM(f"sxb{i}") for i in range(3)]
    sdq2 = [SEM(f"sdq{i}") for i in range(2)]
    sdw2 = [SEM(f"sdw{i}") for i in range(2)]
    ssp2 = [SEM(f"ssp{i}") for i in range(2)]
    sgu2 = [SEM(f"sgu{i}") for i in range(2)]
    swd2 = [SEM(f"swd{i}") for i in range(2)]
    sei4 = [SEM(f"sei{i}") for i in range(3)]
    sso4 = [SEM(f"sso{i}") for i in range(3)]
    sag = [SEM(f"sag{i}") for i in range(NSB)]
    s_ar = SEM("s_ar"); s_arc = SEM("s_arc")
    s_c1 = SEM("s_c1"); s_c2 = SEM("s_c2")
    s_ms1 = SEM("s_ms1"); s_ms2 = SEM("s_ms2")
    s_r1 = SEM("s_r1"); s_r2 = SEM("s_r2")
    s_bm1 = SEM("s_bm1"); s_bm2 = SEM("s_bm2")
    s_b1 = SEM("s_b1"); s_b2 = SEM("s_b2")
    s_sw2 = SEM("s_sw2"); s_sc = SEM("s_sc"); s_sm = SEM("s_sm")
    s_u3 = SEM("s_u3"); s_x1c = SEM("s_x1c"); s_xb = SEM("s_xb")
    s_dq = SEM("s_dq")
    s_m1 = SEM("s_m1"); s_sg = SEM("s_sg"); s_ep = SEM("s_ep")
    s_m2 = SEM("s_m2"); s_eo = SEM("s_eo")
    s_one = SEM("s_one")

    pb = [st.enter_context(nc.psum_tensor(f"pb{i}", [128, 512], F32)) for i in range(8)]

    blk = st.enter_context(nc.Block())

    xb_uses = [0, 0, 0]
    sp_uses = [0, 0]
    dw_uses = [0, 0]
    gu_uses = [0, 0]
    wd_uses = [0, 0]
    ei_uses = [0, 0, 0]
    so_uses = [0, 0, 0]
    ei_ready = {}
    so_ready = {}

    deq_tiles = []
    for s, sb in enumerate(SBS):
        msb = sb * 128
        per = []
        for i in range(4):
            for w in range(msb // DQW):
                per.append(dict(mat='g', sb=s, i=i, w=w))
                per.append(dict(mat='u', sb=s, i=i, w=w))
        for j in range(sb // 8):
            for w in range(H // DQW):
                per.append(dict(mat='d', sb=s, j=j, w=w))
        deq_tiles.append(per)
    n_tiles_sb = [len(p) for p in deq_tiles]
    flat_tiles = [t for p in deq_tiles for t in p]
    NT = len(flat_tiles)

    drow = {}
    racc = 0
    for s, sb in enumerate(SBS):
        for j in range(sb // 8):
            drow[(s, j)] = racc
            racc += 1

    def deq_tile_loads(pool, n):
        t = flat_tiles[n]
        slot = n % 2
        if n >= 2:
            pool.wait_ge(s_dq, n - 1)
        if t['mat'] in ('g', 'u'):
            qsrc = gq if t['mat'] == 'g' else uq
            asrc = gam if t['mat'] == 'g' else uam
            m0 = SBOFF[t['sb']] * 128 + t['w'] * DQW
            i = t['i']
            pool.dma_start(out=dqq[slot][:], in_=qsrc[i * 128:(i + 1) * 128, m0:m0 + DQW]).then_inc(sdq2[slot], 16)
            a0 = asrc[2 * i:2 * i + 1, m0:m0 + DQW]
            a1 = asrc[2 * i + 1:2 * i + 2, m0:m0 + DQW]
        else:
            j = t['j']; r = drow[(t['sb'], j)]
            h0 = t['w'] * DQW
            pool.dma_start(out=dqq[slot][:], in_=dq[r * 128:(r + 1) * 128, h0:h0 + DQW]).then_inc(sdq2[slot], 16)
            a0 = dam[2 * r:2 * r + 1, h0:h0 + DQW]
            a1 = dam[2 * r + 1:2 * r + 2, h0:h0 + DQW]
        pool.dma_start(out=dqa[slot][0:64, :], in_=a0.broadcast_to([64, DQW])).then_inc(sdq2[slot], 16)
        pool.dma_start(out=dqa[slot][64:128, :], in_=a1.broadcast_to([64, DQW])).then_inc(sdq2[slot], 16)

    def deq_tile_compute(vector, n):
        slot = n % 2
        vector.wait_ge(sdq2[slot], 48 * (n // 2 + 1))
        if n >= 2:
            vector.wait_ge(sdw2[slot], 16 * ((n - 2) // 2 + 1))
        q = dqq[slot][:]
        first = True
        for k in range(16):
            if k == 7:
                continue
            if first:
                vector.tensor_scalar(dqc[:], q, float(k), NF4[k], ALU.is_equal, ALU.mult)
                first = False
            else:
                vector.tensor_scalar(dqt[:], q, float(k), NF4[k], ALU.is_equal, ALU.mult)
                vector.tensor_tensor(dqc[:], dqc[:], dqt[:], ALU.add)
        vector.tensor_tensor(dqw[slot][:], dqc[:], dqa[slot][:], ALU.mult).then_inc(s_dq, 1)

    def deq_tile_write(pool, n):
        t = flat_tiles[n]
        slot = n % 2
        pool.wait_ge(s_dq, n + 1)
        if t['mat'] in ('g', 'u'):
            dst = (ag_g_i if t['mat'] == 'g' else ag_u_i)[t['sb']]
            dv = dst.ap().rearrange("(mb p) (i m) -> p mb i m", p=128, i=4)
            sv = dqw[slot][:].rearrange("p (mb m) -> p mb m", m=128)
            pool.dma_start(out=dv[:, t['w'] * 8:(t['w'] + 1) * 8, t['i'], :],
                           in_=sv).then_inc(sdw2[slot], 16)
        else:
            dst = ag_d_i[t['sb']]
            pool.dma_start(out=dst[t['j'] * 128:(t['j'] + 1) * 128, t['w'] * DQW:(t['w'] + 1) * DQW],
                           in_=dqw[slot][:]).then_inc(sdw2[slot], 16)
        dw_uses[slot] += 1

    def pool_deq_sb(pool, s):
        base = sum(n_tiles_sb[:s])
        for n in range(base, base + n_tiles_sb[s]):
            deq_tile_loads(pool, n)
            deq_tile_write(pool, n)

    def pool_ag_sb(pool, s):
        pool.wait_ge(sdw2[0], 16 * dw_uses[0])
        pool.wait_ge(sdw2[1], 16 * dw_uses[1])
        for (ti, to) in ((ag_g_i[s], ag_g_o[s]), (ag_u_i[s], ag_u_o[s]), (ag_d_i[s], ag_d_o[s])):
            pool.collective_compute(
                "AllGather", ALU.bypass,
                replica_groups=[list(range(NCORE))],
                ins=[ti.ap().opt()], outs=[to.ap().opt()],
            ).then_inc(sag[s], 1)

    at = ExitStack()
    Ta = lambda *a, **k: at.enter_context(nc.sbuf_tensor(*a, **k))
    xbuf = [Ta(f"xbuf{i}", [128, 1024], F32) for i in range(3)]
    xcb = [Ta(f"xcb{i}", [128, 1024], BF16) for i in range(2)]
    x2b = [Ta(f"x2b{i}", [128, 1024], BF16) for i in range(2)]
    t1b = Ta("t1b", [128, 1024], F32)
    h1b = Ta("h1b", [128, 1024], F32)
    macc = Ta("macc", [128, 1024], F32)
    x1c = [Ta(f"x1c{i}", [128, 1024], F32) for i in range(2)]

    xb_ready = {}

    def xb_load(sync, sweep, c, src, gate_sem=None, gate_val=None):
        slot = c % 3
        if gate_sem is not None:
            sync.wait_ge(gate_sem, gate_val)
        sync.dma_start(out=xbuf[slot][:], in_=src[c * 128:(c + 1) * 128, :]).then_inc(sxb[slot], 16)
        xb_uses[slot] += 1
        xb_ready[(sweep, c)] = 16 * xb_uses[slot]

    @blk.sync
    def _(sync):
        for c in range(NCH):
            if c >= 3:
                xb_load(sync, 1, c, xT, s_c1, c - 3 + 1)
            else:
                xb_load(sync, 1, c, xT)

    @blk.scalar
    def _(scalar):
        for c in range(NCH):
            scalar.wait_ge(sxb[c % 3], xb_ready[(1, c)])
            if c >= 2:
                scalar.wait_ge(s_one, c - 1)
            scalar.activation(xcb[c % 2][:], xbuf[c % 3][:], AF.Copy)
            scalar.activation(x2b[c % 2][:], xbuf[c % 3][:], AF.Square).then_inc(s_c1, 1)

    @blk.vector
    def _(vector):
        vector.memset(ones[:], 1.0)
        vector.memset(onesf[:], 1.0)
        for n in range(n_tiles_sb[0]):
            deq_tile_compute(vector, n)

    @blk.tensor
    def _(tensor):
        for c in range(NCH):
            tensor.wait_ge(s_c1, c + 1)
            first = (c == 0); last = (c == NCH - 1)
            mm1 = tensor.matmul(pb[4][0:1, :], ones[:], xcb[c % 2][:, 0:512], start=first, stop=last)
            mm2 = tensor.matmul(pb[5][0:1, :], ones[:], xcb[c % 2][:, 512:1024], start=first, stop=last)
            mm3 = tensor.matmul(pb[6][0:1, :], ones[:], x2b[c % 2][:, 0:512], start=first, stop=last)
            mm4 = tensor.matmul(pb[7][0:1, :], ones[:], x2b[c % 2][:, 512:1024], start=first, stop=last)
            if last:
                for mm in (mm1, mm2, mm3, mm4):
                    mm.then_inc(s_ms1, 1)
            else:
                mm4.then_inc(s_one, 1)

    @blk.vector
    def _(vector):
        vector.wait_ge(s_ms1, 4)
        vector.tensor_copy(rowA[:, 0:512], pb[4][0:1, :])
        vector.tensor_copy(rowA[:, 512:1024], pb[5][0:1, :])
        vector.tensor_copy(rowB[:, 0:512], pb[6][0:1, :])
        vector.tensor_copy(rowB[:, 512:1024], pb[7][0:1, :])
        vector.tensor_scalar(rowA[:], rowA[:], 1.0 / H, None, ALU.mult)
        vector.tensor_scalar(rowB[:], rowB[:], 1.0 / H, None, ALU.mult)
        vector.scalar_tensor_tensor(rb[0:1, :], rowA[:], 1.0, rowA[:], ALU.mult, ALU.mult)
        vector.tensor_tensor(rowB[:], rowB[:], rb[0:1, :], ALU.subtract)
        vector.tensor_scalar(rowB[:], rowB[:], EPS, None, ALU.add).then_inc(s_r1, 1)

    @blk.scalar
    def _(scalar):
        scalar.wait_ge(s_r1, 1)
        scalar.activation(rowB[:], rowB[:], AF.Sqrt).then_inc(s_r1, 1)

    @blk.vector
    def _(vector):
        vector.wait_ge(s_r1, 2)
        vector.reciprocal(rowB[:], rowB[:])
        vector.tensor_tensor(rowA[:], rowA[:], rowB[:], ALU.mult).then_inc(s_r1, 1)

    @blk.tensor
    def _(tensor):
        tensor.wait_ge(s_r1, 3)
        tensor.matmul(pb[4][:], onesf[:], rowB[0:1, 0:512], start=True, stop=True)
        tensor.matmul(pb[5][:], onesf[:], rowB[0:1, 512:1024], start=True, stop=True)
        tensor.matmul(pb[6][:], onesf[:], rowA[0:1, 0:512], start=True, stop=True)
        tensor.matmul(pb[7][:], onesf[:], rowA[0:1, 512:1024], start=True, stop=True).then_inc(s_bm1, 1)

    @blk.vector
    def _(vector):
        vector.wait_ge(s_bm1, 1)
        vector.tensor_copy(rb[:, 0:512], pb[4][:])
        vector.tensor_copy(rb[:, 512:1024], pb[5][:])
        vector.tensor_copy(mrb[:, 0:512], pb[6][:])
        vector.tensor_copy(mrb[:, 512:1024], pb[7][:]).then_inc(s_b1, 1)

    @blk.gpsimd
    def _(pool):
        pool_deq_sb(pool, 0)
        pool_ag_sb(pool, 0)

    @blk.sync
    def _(sync):
        for c in range(NCH):
            if c >= 3:
                xb_load(sync, 2, c, xT, s_sw2, c - 3 + 1)
            else:
                xb_load(sync, 2, c, xT)

    @blk.vector
    def _(vector):
        vector.wait_ge(s_b1, 1)
        for c in range(NCH):
            vector.wait_ge(sxb[c % 3], xb_ready[(2, c)])
            xc = xbuf[c % 3][:]
            vector.tensor_tensor(t1b[:], xc, rb[:], ALU.mult)
            vector.tensor_tensor(h1b[:], t1b[:], mrb[:], ALU.subtract).then_inc(s_sw2, 1)
            for p, (a, b2) in enumerate(PAIRS):
                stt = vector.scalar_tensor_tensor(
                    zd[:], h1b[:, a * 256:(a + 1) * 256], 1.0, h1b[:, b2 * 256:(b2 + 1) * 256],
                    ALU.mult, ALU.mult, accum_out=ssb[:, c * 10 + p:c * 10 + p + 1])
                if c == NCH - 1 and p == len(PAIRS) - 1:
                    stt.then_inc(s_sc, 1)

    @blk.gpsimd
    def _(pool):
        pool.wait_ge(s_sc, 1)
        pool.dma_start(out=sar_i[:, :], in_=ssb[:]).then_inc(s_ar, 16)
        pool.wait_ge(s_ar, 16)
        pool.collective_compute(
            "AllReduce", ALU.add,
            replica_groups=[list(range(NCORE))],
            ins=[sar_i.ap().opt()], outs=[sar_o.ap().opt()],
        ).then_inc(s_arc, 1)

    @blk.vector
    def _(vector):
        for n in range(n_tiles_sb[0], n_tiles_sb[0] + n_tiles_sb[1]):
            deq_tile_compute(vector, n)
        vector.wait_ge(s_ar, 32)
        sf = ssb[:].rearrange("p (c t) -> p c t", t=10)
        s163 = s16[:].rearrange("p (c t) -> p c t", t=16)
        for n in range(4):
            for m in range(4):
                vector.tensor_copy(s163[:, :, n * 4 + m:n * 4 + m + 1], sf[:, :, PIDX[(n, m)]:PIDX[(n, m)] + 1])
        s4 = s16[:].rearrange("p (c n m) -> p c n m", n=4, m=4)
        p4 = p16[:].rearrange("p (c n m) -> p c n m", n=4, m=4)
        vector.tensor_reduce(aux4[:].rearrange("p (c n) -> p c n", n=4), s4, mybir.AxisListType.X, ALU.max)
        mxb = aux4[:].rearrange("p (c n one) -> p c n one", n=4, one=1).broadcast_to([128, NCH, 4, 4])
        vector.tensor_tensor(p4, s4, mxb, ALU.subtract).then_inc(s_sm, 1)

    @blk.scalar
    def _(scalar):
        scalar.wait_ge(s_sm, 1)
        scalar.activation(p16[:], p16[:], AF.Exp, scale=SM_SCALE).then_inc(s_sm, 1)

    @blk.vector
    def _(vector):
        vector.wait_ge(s_sm, 2)
        p4 = p16[:].rearrange("p (c n m) -> p c n m", n=4, m=4)
        vector.tensor_reduce(aux4[:].rearrange("p (c n) -> p c n", n=4), p4, mybir.AxisListType.X, ALU.add)
        vector.reciprocal(aux4[:], aux4[:])
        rcb = aux4[:].rearrange("p (c n one) -> p c n one", n=4, one=1).broadcast_to([128, NCH, 4, 4])
        vector.tensor_tensor(p4, p4, rcb, ALU.mult)

    @blk.sync
    def _(sync):
        def spill(cc):
            slot2 = cc % 2
            sync.wait_ge(s_x1c, cc + 1)
            sync.dma_start(out=x1h[cc * 128:(cc + 1) * 128, :], in_=x1c[slot2][:]).then_inc(ssp2[slot2], 16)
            sp_uses[slot2] += 1
        for c in range(NCH):
            if c >= 3:
                xb_load(sync, 3, c, xT, s_u3, c - 3 + 1)
                spill(c - 3)
            else:
                xb_load(sync, 3, c, xT)
        for cc in range(NCH - 3, NCH):
            spill(cc)

    @blk.vector
    def _(vector):
        for c in range(NCH):
            vector.wait_ge(sxb[c % 3], xb_ready[(3, c)])
            if c >= 2:
                vector.wait_ge(s_c2, c - 2 + 1)
                vector.wait_ge(ssp2[c % 2], 16 * ((c - 2) // 2 + 1))
            xc = xbuf[c % 3][:]
            vector.tensor_tensor(t1b[:], xc, rb[:], ALU.mult)
            vector.tensor_tensor(h1b[:], t1b[:], mrb[:], ALU.subtract)
            for n in range(4):
                dst = macc[:, n * 256:(n + 1) * 256]
                for m in range(4):
                    pav = p16[:, c * 16 + n * 4 + m:c * 16 + n * 4 + m + 1]
                    src = h1b[:, m * 256:(m + 1) * 256]
                    if m == 0:
                        vector.tensor_scalar(dst, src, pav, None, ALU.mult)
                    else:
                        vector.scalar_tensor_tensor(dst, src, pav, dst, ALU.mult, ALU.add)
            vector.tensor_tensor(h1b[:], h1b[:], xc, ALU.add).then_inc(s_u3, 1)
            vector.tensor_tensor(x1c[c % 2][:], h1b[:], macc[:], ALU.add).then_inc(s_x1c, 1)

    @blk.scalar
    def _(scalar):
        for c in range(NCH):
            scalar.wait_ge(s_x1c, c + 1)
            if c >= 2:
                scalar.wait_ge(s_one, 31 + c - 1)
            scalar.activation(xcb[c % 2][:], x1c[c % 2][:], AF.Copy)
            scalar.activation(x2b[c % 2][:], x1c[c % 2][:], AF.Square).then_inc(s_c2, 1)

    @blk.tensor
    def _(tensor):
        tensor.wait_ge(s_b1, 1)
        for c in range(NCH):
            tensor.wait_ge(s_c2, c + 1)
            first = (c == 0); last = (c == NCH - 1)
            mm1 = tensor.matmul(pb[4][0:1, :], ones[:], xcb[c % 2][:, 0:512], start=first, stop=last)
            mm2 = tensor.matmul(pb[5][0:1, :], ones[:], xcb[c % 2][:, 512:1024], start=first, stop=last)
            mm3 = tensor.matmul(pb[6][0:1, :], ones[:], x2b[c % 2][:, 0:512], start=first, stop=last)
            mm4 = tensor.matmul(pb[7][0:1, :], ones[:], x2b[c % 2][:, 512:1024], start=first, stop=last)
            if last:
                for mm in (mm1, mm2, mm3, mm4):
                    mm.then_inc(s_ms2, 1)
            else:
                mm4.then_inc(s_one, 1)

    @blk.vector
    def _(vector):
        vector.wait_ge(s_ms2, 4)
        vector.tensor_copy(rowA[:, 0:512], pb[4][0:1, :])
        vector.tensor_copy(rowA[:, 512:1024], pb[5][0:1, :])
        vector.tensor_copy(rowB[:, 0:512], pb[6][0:1, :])
        vector.tensor_copy(rowB[:, 512:1024], pb[7][0:1, :])
        vector.tensor_scalar(rowA[:], rowA[:], 1.0 / H, None, ALU.mult)
        vector.tensor_scalar(rowB[:], rowB[:], 1.0 / H, None, ALU.mult)
        vector.scalar_tensor_tensor(rb[0:1, :], rowA[:], 1.0, rowA[:], ALU.mult, ALU.mult)
        vector.tensor_tensor(rowB[:], rowB[:], rb[0:1, :], ALU.subtract)
        vector.tensor_scalar(rowB[:], rowB[:], EPS, None, ALU.add).then_inc(s_r2, 1)

    @blk.scalar
    def _(scalar):
        scalar.wait_ge(s_r2, 1)
        scalar.activation(rowB[:], rowB[:], AF.Sqrt).then_inc(s_r2, 1)

    @blk.vector
    def _(vector):
        vector.wait_ge(s_r2, 2)
        vector.reciprocal(rowB[:], rowB[:])
        vector.tensor_tensor(rowA[:], rowA[:], rowB[:], ALU.mult).then_inc(s_r2, 1)

    @blk.tensor
    def _(tensor):
        tensor.wait_ge(s_r2, 3)
        tensor.matmul(pb[4][:], onesf[:], rowB[0:1, 0:512], start=True, stop=True)
        tensor.matmul(pb[5][:], onesf[:], rowB[0:1, 512:1024], start=True, stop=True)
        tensor.matmul(pb[6][:], onesf[:], rowA[0:1, 0:512], start=True, stop=True)
        tensor.matmul(pb[7][:], onesf[:], rowA[0:1, 512:1024], start=True, stop=True).then_inc(s_bm2, 1)

    @blk.vector
    def _(vector):
        vector.wait_ge(s_bm2, 1)
        vector.tensor_copy(rb[:, 0:512], pb[4][:])
        vector.tensor_copy(rb[:, 512:1024], pb[5][:])
        vector.tensor_copy(mrb[:, 0:512], pb[6][:])
        vector.tensor_copy(mrb[:, 512:1024], pb[7][:]).then_inc(s_b2, 1)

    @blk.gpsimd
    def _(pool):
        pool_deq_sb(pool, 1)
        pool_ag_sb(pool, 1)
        pool.wait_ge(s_arc, 1)
        pool.dma_start(out=ssb[:], in_=sar_o[:, :]).then_inc(s_ar, 16)

    @blk.sync
    def _(sync):
        for c in range(NCH):
            sync.wait_ge(ssp2[c % 2], 16 * (c // 2 + 1))
            if c >= 3:
                xb_load(sync, 4, c, x1h, s_xb, c - 3 + 1)
            else:
                xb_load(sync, 4, c, x1h)

    @blk.vector
    def _(vector):
        vector.wait_ge(s_b2, 1)
        for c in range(NCH):
            vector.wait_ge(sxb[c % 3], xb_ready[(4, c)])
            xc = xbuf[c % 3][:]
            vector.tensor_tensor(t1b[:], xc, rb[:], ALU.mult)
            vector.tensor_tensor(xbT[:, c, :], t1b[:], mrb[:], ALU.subtract).then_inc(s_xb, 1)

    at.close()

    ff = ExitStack()
    Tf = lambda *a, **k: ff.enter_context(nc.sbuf_tensor(*a, **k))
    MAXSB = max(SBS)
    hsw = Tf("hsw", [128, MAXSB, 1024], BF16)
    gu = [Tf(f"gu{i}", [128, 2, NCORE, 4, 128], BF16) for i in range(2)]
    wdt = [Tf(f"wdt{i}", [128, MAXSB, 256], BF16) for i in range(2)]
    sg = [Tf(f"sg{i}", [128, 512], F32) for i in range(2)]
    ug = [Tf(f"ug{i}", [128, 512], F32) for i in range(2)]
    evi = [Tf(f"evi{i}", [128, 512], F32) for i in range(3)]
    evo = [Tf(f"evo{i}", [128, 512], F32) for i in range(3)]

    NEV = 64

    @blk.sync
    def _(sync):
        gu_cnt = 0
        wd_cnt = 0
        ei_cnt = 0
        eo_cnt = 0
        for s, sb in enumerate(SBS):
            for j in range(sb):
                slot = gu_cnt % 2
                if gu_cnt >= 2:
                    sync.wait_ge(s_m1, 4 * (gu_cnt - 1))
                sync.wait_ge(sag[s], 3)
                gsrc = ag_g_o[s].ap().rearrange("(r mb p) im -> p r mb im", p=128, mb=sb)
                sync.dma_start(out=gu[slot][:, 0, :, :, :],
                               in_=gsrc[:, :, j, :].rearrange("p r (i m) -> p r i m", i=4)).then_inc(sgu2[slot], 16)
                usrc = ag_u_o[s].ap().rearrange("(r mb p) im -> p r mb im", p=128, mb=sb)
                sync.dma_start(out=gu[slot][:, 1, :, :, :],
                               in_=usrc[:, :, j, :].rearrange("p r (i m) -> p r i m", i=4)).then_inc(sgu2[slot], 16)
                gu_uses[slot] += 1
                gu_cnt += 1
            def evac_pair(hb):
                nonlocal ei_cnt, eo_cnt
                for th in range(2):
                    tile = hb * 2 + th
                    slot = (NEV * s + tile) % 3
                    if ei_cnt >= 3:
                        sync.wait_ge(s_eo, ei_cnt - 2)
                    if s == 0:
                        sync.wait_ge(ssp2[hb % 2], 16 * (hb // 2 + 1))
                        src = x1h
                    else:
                        sync.wait_ge(sso4[tile % 3 if False else (NEV * (s - 1) + tile) % 3],
                                     so_ready[(s - 1, tile)])
                        src = parth
                    sync.dma_start(out=evi[slot][:],
                                   in_=src[hb * 128:(hb + 1) * 128, th * 512:(th + 1) * 512]).then_inc(sei4[slot], 16)
                    ei_uses[slot] += 1
                    ei_ready[(s, tile)] = 16 * ei_uses[slot]
                    ei_cnt += 1
                    sync.wait_ge(s_eo, NEV * s + tile + 1)
                    dstt = outT if s == NSB - 1 else parth
                    sync.dma_start(out=dstt[hb * 128:(hb + 1) * 128, th * 512:(th + 1) * 512],
                                   in_=evo[slot][:]).then_inc(sso4[slot], 16)
                    so_uses[slot] += 1
                    so_ready[(s, tile)] = 16 * so_uses[slot]
                    eo_cnt += 1
            for hbg in range(NCH // 2):
                slot = wd_cnt % 2
                if wd_cnt >= 2:
                    sync.wait_ge(s_m2, 4 * (wd_cnt - 1))
                dsrc = ag_d_o[s].ap().rearrange("(j p) h -> p j h", p=128)
                sync.dma_start(out=wdt[slot][:, 0:sb, :],
                               in_=dsrc[:, :, hbg * 256:(hbg + 1) * 256]).then_inc(swd2[slot], 16)
                wd_uses[slot] += 1
                wd_cnt += 1
                if hbg >= 1:
                    evac_pair(2 * hbg - 2)
                    evac_pair(2 * hbg - 1)
            evac_pair(NCH - 2)
            evac_pair(NCH - 1)
        for j in range(3):
            sync.wait_ge(sso4[j], 16 * so_uses[j])

    @blk.tensor
    def _(tensor):
        tensor.wait_ge(s_xb, NCH)
        gu_cnt = 0
        wd_cnt = 0
        jth = 0
        m2g = 0
        for s, sb in enumerate(SBS):
            for j in range(sb):
                slot = gu_cnt % 2
                gu_cnt += 1
                tensor.wait_ge(sgu2[slot], 32 * ((gu_cnt - 1) // 2 + 1))
                for th in range(2):
                    par = jth % 2
                    if jth >= 2:
                        tensor.wait_ge(s_sg, 2 * (jth - 2) + 2)
                    pg = pb[0 + 2 * par]
                    pu = pb[1 + 2 * par]
                    for hc in range(NCH):
                        mm = tensor.matmul(pg[:], gu[slot][:, 0, hc // 4, hc % 4, :],
                                           xbT[:, hc, th * 512:(th + 1) * 512],
                                           start=(hc == 0), stop=(hc == NCH - 1))
                    mm.then_inc(s_m1, 1)
                    for hc in range(NCH):
                        mm = tensor.matmul(pu[:], gu[slot][:, 1, hc // 4, hc % 4, :],
                                           xbT[:, hc, th * 512:(th + 1) * 512],
                                           start=(hc == 0), stop=(hc == NCH - 1))
                    mm.then_inc(s_m1, 1)
                    jth += 1
            tensor.wait_ge(s_ep, jth)
            for hbg in range(NCH // 2):
                slot = wd_cnt % 2
                wd_cnt += 1
                tensor.wait_ge(swd2[slot], 16 * ((wd_cnt - 1) // 2 + 1))
                for sub in range(2):
                    for th in range(2):
                        po = pb[4 + (m2g % 4)]
                        if m2g >= 4:
                            tensor.wait_ge(s_eo, m2g - 3)
                        for jj in range(sb):
                            mm = tensor.matmul(po[:], wdt[slot][:, jj, sub * 128:(sub + 1) * 128],
                                               hsw[:, jj, th * 512:(th + 1) * 512],
                                               start=(jj == 0), stop=(jj == sb - 1))
                        mm.then_inc(s_m2, 1)
                        m2g += 1

    @blk.scalar
    def _(scalar):
        jth = 0
        for s, sb in enumerate(SBS):
            for j in range(sb):
                for th in range(2):
                    par = jth % 2
                    if jth >= 2:
                        scalar.wait_ge(s_ep, jth - 1)
                    scalar.wait_ge(s_m1, 2 * jth + 1)
                    scalar.activation(sg[par][:], pb[0 + 2 * par][:], AF.Silu).then_inc(s_sg, 1)
                    scalar.wait_ge(s_m1, 2 * jth + 2)
                    scalar.activation(ug[par][:], pb[1 + 2 * par][:], AF.Copy).then_inc(s_sg, 1)
                    jth += 1

    @blk.vector
    def _(vector):
        base2 = n_tiles_sb[0] + n_tiles_sb[1]
        for n in range(base2, base2 + n_tiles_sb[2]):
            deq_tile_compute(vector, n)
        deq_ptr = base2 + n_tiles_sb[2]
        jth = 0
        m2g = 0
        for s, sb in enumerate(SBS):
            for j in range(sb):
                for th in range(2):
                    par = jth % 2
                    vector.wait_ge(s_sg, 2 * jth + 2)
                    vector.tensor_tensor(hsw[:, j, th * 512:(th + 1) * 512], sg[par][:],
                                         ug[par][:], ALU.mult).then_inc(s_ep, 1)
                    jth += 1
                    if deq_ptr < NT:
                        deq_tile_compute(vector, deq_ptr)
                        deq_ptr += 1
            for hb in range(NCH):
                for th in range(2):
                    tile = hb * 2 + th
                    g = NEV * s + tile
                    slot = g % 3
                    vector.wait_ge(s_m2, m2g + 1)
                    vector.wait_ge(sei4[slot], ei_ready[(s, tile)])
                    if g >= 3:
                        gp = g - 3
                        vector.wait_ge(sso4[slot], so_ready[(gp // NEV, gp % NEV)])
                    po = pb[4 + (m2g % 4)]
                    vector.tensor_tensor(evo[slot][:], po[:], evi[slot][:], ALU.add).then_inc(s_eo, 1)
                    m2g += 1
        while deq_ptr < NT:
            deq_tile_compute(vector, deq_ptr)
            deq_ptr += 1

    @blk.gpsimd
    def _(pool):
        for s in range(2, NSB):
            pool_deq_sb(pool, s)
            pool_ag_sb(pool, s)

    ff.close()
    st.close()
    return nc


_CACHED = {}


def kernel(L0_attn_x, L0_ffn_gate_q, L0_ffn_gate_absmax, L0_ffn_up_q, L0_ffn_up_absmax,
           L0_ffn_down_q, L0_ffn_down_absmax):
    x = np.asarray(L0_attn_x)
    gq_full = np.ascontiguousarray(np.asarray(L0_ffn_gate_q).astype(np.int16).T)
    uq_full = np.ascontiguousarray(np.asarray(L0_ffn_up_q).astype(np.int16).T)
    dq_full = np.ascontiguousarray(np.asarray(L0_ffn_down_q).astype(np.int16).T)
    gam_full = np.ascontiguousarray(np.asarray(L0_ffn_gate_absmax).reshape(M, 64).T)
    uam_full = np.ascontiguousarray(np.asarray(L0_ffn_up_absmax).reshape(M, 64).T)
    dam_full = np.ascontiguousarray(np.asarray(L0_ffn_down_absmax).reshape(H, M // 64).T)

    if 'nc' not in _CACHED:
        _CACHED['nc'] = build_program()
    nc = _CACHED['nc']

    in_maps = []
    for k in range(NCORE):
        xk = np.ascontiguousarray(x[:, CLOC * k:CLOC * (k + 1), :].reshape(TLOC, H).T)
        drows = []
        damrows = []
        for s, sb in enumerate(SBS):
            for j in range(sb // 8):
                mb = SBOFF[s] + k * (sb // 8) + j
                drows.append(dq_full[mb * 128:(mb + 1) * 128, :])
                damrows.append(dam_full[2 * mb:2 * mb + 2, :])
        in_maps.append({
            "xT": xk,
            "gq": np.ascontiguousarray(gq_full[512 * k:512 * (k + 1), :]),
            "uq": np.ascontiguousarray(uq_full[512 * k:512 * (k + 1), :]),
            "gam": np.ascontiguousarray(gam_full[8 * k:8 * (k + 1), :]),
            "uam": np.ascontiguousarray(uam_full[8 * k:8 * (k + 1), :]),
            "dq": np.ascontiguousarray(np.concatenate(drows, axis=0)),
            "dam": np.ascontiguousarray(np.concatenate(damrows, axis=0)),
        })

    res = run_bass_kernel_spmd(nc, in_maps, core_ids=list(range(NCORE)))
    out = np.empty((B, CSEQ, H), dtype=np.float32)
    for k in range(NCORE):
        ok = np.asarray(res.results[k]["outT"])
        out[:, CLOC * k:CLOC * (k + 1), :] = ok.T.reshape(B, CLOC, H)
    return out


# version 06
# speedup vs baseline: 1.2197x; 1.2197x over previous
import numpy as np
import ml_dtypes
import concourse.bass as bass
import concourse.mybir as mybir
from concourse.bass_utils import run_bass_kernel_spmd
from contextlib import ExitStack

F32 = mybir.dt.float32
BF16 = mybir.dt.bfloat16
I16 = mybir.dt.int16
AF = mybir.ActivationFunctionType
ALU = mybir.AluOpType

NF4 = [-1.0, -0.6961928009986877, -0.5250730514526367, -0.39491748809814453,
       -0.28444138169288635, -0.18477343022823334, -0.09105003625154495, 0.0,
       0.07958029955625534, 0.16093020141124725, 0.24611230194568634, 0.33791524171829224,
       0.44070982933044434, 0.5626170039176941, 0.7229568362236023, 1.0]

H = 4096
M = 14336
B = 4
CSEQ = 2048
NCORE = 8
TLOC = 1024
CLOC = 256
NCH = 32
SBS = [8, 8, 16, 16, 16, 16, 16, 16]
SBOFF = [0, 8, 16, 32, 48, 64, 80, 96]
NSB = len(SBS)
DQW = 1024
EPS = 1e-5
SM_SCALE = 1.0 / float(np.sqrt(np.float32(CSEQ)))
PAIRS = [(0, 0), (0, 1), (0, 2), (0, 3), (1, 1), (1, 2), (1, 3), (2, 2), (2, 3), (3, 3)]
PIDX = {}
for _i, (_a, _b2) in enumerate(PAIRS):
    PIDX[(_a, _b2)] = _i
    PIDX[(_b2, _a)] = _i


def build_program():
    nc = bass.Bass("TRN2", target_bir_lowering=False)

    xT = nc.dram_tensor("xT", [H, TLOC], F32, kind="ExternalInput").ap()
    gq = nc.dram_tensor("gq", [512, M], I16, kind="ExternalInput").ap()
    uq = nc.dram_tensor("uq", [512, M], I16, kind="ExternalInput").ap()
    gam = nc.dram_tensor("gam", [8, M], F32, kind="ExternalInput").ap()
    uam = nc.dram_tensor("uam", [8, M], F32, kind="ExternalInput").ap()
    dq = nc.dram_tensor("dq", [14 * 128, H], I16, kind="ExternalInput").ap()
    dam = nc.dram_tensor("dam", [14 * 2, H], F32, kind="ExternalInput").ap()
    outT = nc.dram_tensor("outT", [H, TLOC], F32, kind="ExternalOutput").ap()

    x1h = nc.dram_tensor("x1h", [H, TLOC], F32, kind="Internal").ap()
    parth = nc.dram_tensor("parth", [H, TLOC], F32, kind="Internal").ap()
    sar_i = nc.dram_tensor("sar_i", [128, 320], F32, kind="Internal")
    sar_o = nc.dram_tensor("sar_o", [128, 320], F32, kind="Internal")
    ag_g_i, ag_g_o, ag_u_i, ag_u_o, ag_d_i, ag_d_o = [], [], [], [], [], []
    for s, sb in enumerate(SBS):
        msb = sb * 128
        ag_g_i.append(nc.dram_tensor(f"aggi{s}", [sb * 128, 512], BF16, kind="Internal"))
        ag_g_o.append(nc.dram_tensor(f"aggo{s}", [NCORE * sb * 128, 512], BF16, kind="Internal"))
        ag_u_i.append(nc.dram_tensor(f"agui{s}", [sb * 128, 512], BF16, kind="Internal"))
        ag_u_o.append(nc.dram_tensor(f"aguo{s}", [NCORE * sb * 128, 512], BF16, kind="Internal"))
        ag_d_i.append(nc.dram_tensor(f"agdi{s}", [(sb // 8) * 128, H], BF16, kind="Internal"))
        ag_d_o.append(nc.dram_tensor(f"agdo{s}", [sb * 128, H], BF16, kind="Internal"))

    st = ExitStack()
    T = lambda *a, **k: st.enter_context(nc.sbuf_tensor(*a, **k))
    SEM = lambda n: st.enter_context(nc.semaphore(n))

    xbT = T("xbT", [128, NCH, 1024], BF16)
    dqq = [T(f"dqq{i}", [128, DQW], I16) for i in range(2)]
    dqa = [T(f"dqa{i}", [128, DQW], F32) for i in range(2)]
    dqc = T("dqc", [128, DQW], BF16)
    dqt = T("dqt", [128, DQW], BF16)
    dqw = [T(f"dqw{i}", [128, DQW], BF16) for i in range(2)]
    ssb = T("ssb", [128, 320], F32)
    s16 = T("s16", [128, 512], F32)
    p16 = T("p16", [128, 512], F32)
    aux4 = T("aux4", [128, 128], F32)
    zd = T("zd", [128, 256], F32)
    rb = T("rb", [128, 1024], F32)
    mrb = T("mrb", [128, 1024], F32)
    rowA = T("rowA", [1, 1024], F32)
    rowB = T("rowB", [1, 1024], F32)
    onesf = T("onesf", [1, 128], F32)
    ones = T("ones", [128, 1], BF16)

    sxb = [SEM(f"sxb{i}") for i in range(3)]
    sdq2 = [SEM(f"sdq{i}") for i in range(2)]
    sdw2 = [SEM(f"sdw{i}") for i in range(2)]
    ssp2 = [SEM(f"ssp{i}") for i in range(2)]
    sgu2 = [SEM(f"sgu{i}") for i in range(2)]
    swd2 = [SEM(f"swd{i}") for i in range(2)]
    sei4 = [SEM(f"sei{i}") for i in range(3)]
    sso4 = [SEM(f"sso{i}") for i in range(3)]
    sag = [SEM(f"sag{i}") for i in range(NSB)]
    s_ar = SEM("s_ar"); s_arc = SEM("s_arc")
    s_c1 = SEM("s_c1"); s_c2 = SEM("s_c2")
    s_ms1 = SEM("s_ms1"); s_ms2 = SEM("s_ms2")
    s_r1 = SEM("s_r1"); s_r2 = SEM("s_r2")
    s_bm1 = SEM("s_bm1"); s_bm2 = SEM("s_bm2")
    s_b1 = SEM("s_b1"); s_b2 = SEM("s_b2")
    s_sw2 = SEM("s_sw2"); s_sc = SEM("s_sc"); s_sm = SEM("s_sm")
    s_u3 = SEM("s_u3"); s_x1c = SEM("s_x1c"); s_xb = SEM("s_xb")
    s_dq = SEM("s_dq")
    s_m1 = SEM("s_m1"); s_sg = SEM("s_sg"); s_ep = SEM("s_ep")
    s_m2 = SEM("s_m2"); s_eo = SEM("s_eo")
    s_one = SEM("s_one")

    pb = [st.enter_context(nc.psum_tensor(f"pb{i}", [128, 512], F32)) for i in range(8)]

    blk = st.enter_context(nc.Block())

    xb_uses = [0, 0, 0]
    sp_uses = [0, 0]
    dw_uses = [0, 0]
    gu_uses = [0, 0]
    wd_uses = [0, 0]
    ei_uses = [0, 0, 0]
    so_uses = [0, 0, 0]
    ei_ready = {}
    so_ready = {}

    deq_tiles = []
    for s, sb in enumerate(SBS):
        msb = sb * 128
        per = []
        for i in range(4):
            for w in range(msb // DQW):
                per.append(dict(mat='g', sb=s, i=i, w=w))
                per.append(dict(mat='u', sb=s, i=i, w=w))
        for j in range(sb // 8):
            for w in range(H // DQW):
                per.append(dict(mat='d', sb=s, j=j, w=w))
        deq_tiles.append(per)
    n_tiles_sb = [len(p) for p in deq_tiles]
    flat_tiles = [t for p in deq_tiles for t in p]
    NT = len(flat_tiles)

    drow = {}
    racc = 0
    for s, sb in enumerate(SBS):
        for j in range(sb // 8):
            drow[(s, j)] = racc
            racc += 1

    def deq_tile_loads(pool, n):
        t = flat_tiles[n]
        slot = n % 2
        if n >= 2:
            pool.wait_ge(s_dq, n - 1)
        if t['mat'] in ('g', 'u'):
            qsrc = gq if t['mat'] == 'g' else uq
            asrc = gam if t['mat'] == 'g' else uam
            m0 = SBOFF[t['sb']] * 128 + t['w'] * DQW
            i = t['i']
            pool.dma_start(out=dqq[slot][:], in_=qsrc[i * 128:(i + 1) * 128, m0:m0 + DQW]).then_inc(sdq2[slot], 16)
            a0 = asrc[2 * i:2 * i + 1, m0:m0 + DQW]
            a1 = asrc[2 * i + 1:2 * i + 2, m0:m0 + DQW]
        else:
            j = t['j']; r = drow[(t['sb'], j)]
            h0 = t['w'] * DQW
            pool.dma_start(out=dqq[slot][:], in_=dq[r * 128:(r + 1) * 128, h0:h0 + DQW]).then_inc(sdq2[slot], 16)
            a0 = dam[2 * r:2 * r + 1, h0:h0 + DQW]
            a1 = dam[2 * r + 1:2 * r + 2, h0:h0 + DQW]
        pool.dma_start(out=dqa[slot][0:64, :], in_=a0.broadcast_to([64, DQW])).then_inc(sdq2[slot], 16)
        pool.dma_start(out=dqa[slot][64:128, :], in_=a1.broadcast_to([64, DQW])).then_inc(sdq2[slot], 16)

    def deq_tile_compute(vector, n):
        slot = n % 2
        vector.wait_ge(sdq2[slot], 48 * (n // 2 + 1))
        if n >= 2:
            vector.wait_ge(sdw2[slot], 16 * ((n - 2) // 2 + 1))
        q = dqq[slot][:]
        first = True
        for k in range(16):
            if k == 7:
                continue
            if first:
                vector.tensor_scalar(dqc[:], q, float(k), NF4[k], ALU.is_equal, ALU.mult)
                first = False
            else:
                vector.tensor_scalar(dqt[:], q, float(k), NF4[k], ALU.is_equal, ALU.mult)
                vector.tensor_tensor(dqc[:], dqc[:], dqt[:], ALU.add)
        vector.tensor_tensor(dqw[slot][:], dqc[:], dqa[slot][:], ALU.mult).then_inc(s_dq, 1)

    def deq_tile_write(pool, n):
        t = flat_tiles[n]
        slot = n % 2
        pool.wait_ge(s_dq, n + 1)
        if t['mat'] in ('g', 'u'):
            dst = (ag_g_i if t['mat'] == 'g' else ag_u_i)[t['sb']]
            dv = dst.ap().rearrange("(mb p) (i m) -> p mb i m", p=128, i=4)
            sv = dqw[slot][:].rearrange("p (mb m) -> p mb m", m=128)
            pool.dma_start(out=dv[:, t['w'] * 8:(t['w'] + 1) * 8, t['i'], :],
                           in_=sv).then_inc(sdw2[slot], 16)
        else:
            dst = ag_d_i[t['sb']]
            pool.dma_start(out=dst[t['j'] * 128:(t['j'] + 1) * 128, t['w'] * DQW:(t['w'] + 1) * DQW],
                           in_=dqw[slot][:]).then_inc(sdw2[slot], 16)
        dw_uses[slot] += 1

    def pool_deq_sb(pool, s):
        base = sum(n_tiles_sb[:s])
        for n in range(base, base + n_tiles_sb[s]):
            deq_tile_loads(pool, n)
            deq_tile_write(pool, n)

    def pool_ag_sb(pool, s):
        pool.wait_ge(sdw2[0], 16 * dw_uses[0])
        pool.wait_ge(sdw2[1], 16 * dw_uses[1])
        for (ti, to) in ((ag_g_i[s], ag_g_o[s]), (ag_u_i[s], ag_u_o[s]), (ag_d_i[s], ag_d_o[s])):
            pool.collective_compute(
                "AllGather", ALU.bypass,
                replica_groups=[list(range(NCORE))],
                ins=[ti.ap().opt()], outs=[to.ap().opt()],
            ).then_inc(sag[s], 1)

    at = ExitStack()
    Ta = lambda *a, **k: at.enter_context(nc.sbuf_tensor(*a, **k))
    xbuf = [Ta(f"xbuf{i}", [128, 1024], F32) for i in range(3)]
    xcb = [Ta(f"xcb{i}", [128, 1024], BF16) for i in range(2)]
    x2b = [Ta(f"x2b{i}", [128, 1024], BF16) for i in range(2)]
    t1b = Ta("t1b", [128, 1024], F32)
    h1b = Ta("h1b", [128, 1024], F32)
    macc = Ta("macc", [128, 1024], F32)
    x1c = [Ta(f"x1c{i}", [128, 1024], F32) for i in range(2)]

    xb_ready = {}

    def xb_load(sync, sweep, c, src, gate_sem=None, gate_val=None):
        slot = c % 3
        if gate_sem is not None:
            sync.wait_ge(gate_sem, gate_val)
        sync.dma_start(out=xbuf[slot][:], in_=src[c * 128:(c + 1) * 128, :]).then_inc(sxb[slot], 16)
        xb_uses[slot] += 1
        xb_ready[(sweep, c)] = 16 * xb_uses[slot]

    @blk.sync
    def _(sync):
        for c in range(NCH):
            if c >= 3:
                xb_load(sync, 1, c, xT, s_c1, c - 3 + 1)
            else:
                xb_load(sync, 1, c, xT)

    @blk.scalar
    def _(scalar):
        for c in range(NCH):
            scalar.wait_ge(sxb[c % 3], xb_ready[(1, c)])
            if c >= 2:
                scalar.wait_ge(s_one, c - 1)
            scalar.activation(xcb[c % 2][:], xbuf[c % 3][:], AF.Copy)
            scalar.activation(x2b[c % 2][:], xbuf[c % 3][:], AF.Square).then_inc(s_c1, 1)

    @blk.vector
    def _(vector):
        vector.memset(ones[:], 1.0)
        vector.memset(onesf[:], 1.0)
        for n in range(n_tiles_sb[0]):
            deq_tile_compute(vector, n)

    @blk.tensor
    def _(tensor):
        for c in range(NCH):
            tensor.wait_ge(s_c1, c + 1)
            first = (c == 0); last = (c == NCH - 1)
            mm1 = tensor.matmul(pb[4][0:1, :], ones[:], xcb[c % 2][:, 0:512], start=first, stop=last)
            mm2 = tensor.matmul(pb[5][0:1, :], ones[:], xcb[c % 2][:, 512:1024], start=first, stop=last)
            mm3 = tensor.matmul(pb[6][0:1, :], ones[:], x2b[c % 2][:, 0:512], start=first, stop=last)
            mm4 = tensor.matmul(pb[7][0:1, :], ones[:], x2b[c % 2][:, 512:1024], start=first, stop=last)
            if last:
                for mm in (mm1, mm2, mm3, mm4):
                    mm.then_inc(s_ms1, 1)
            else:
                mm4.then_inc(s_one, 1)

    @blk.vector
    def _(vector):
        vector.wait_ge(s_ms1, 4)
        vector.tensor_copy(rowA[:, 0:512], pb[4][0:1, :])
        vector.tensor_copy(rowA[:, 512:1024], pb[5][0:1, :])
        vector.tensor_copy(rowB[:, 0:512], pb[6][0:1, :])
        vector.tensor_copy(rowB[:, 512:1024], pb[7][0:1, :])
        vector.tensor_scalar(rowA[:], rowA[:], 1.0 / H, None, ALU.mult)
        vector.tensor_scalar(rowB[:], rowB[:], 1.0 / H, None, ALU.mult)
        vector.scalar_tensor_tensor(rb[0:1, :], rowA[:], 1.0, rowA[:], ALU.mult, ALU.mult)
        vector.tensor_tensor(rowB[:], rowB[:], rb[0:1, :], ALU.subtract)
        vector.tensor_scalar(rowB[:], rowB[:], EPS, None, ALU.add).then_inc(s_r1, 1)

    @blk.scalar
    def _(scalar):
        scalar.wait_ge(s_r1, 1)
        scalar.activation(rowB[:], rowB[:], AF.Sqrt).then_inc(s_r1, 1)

    @blk.vector
    def _(vector):
        vector.wait_ge(s_r1, 2)
        vector.reciprocal(rowB[:], rowB[:])
        vector.tensor_tensor(rowA[:], rowA[:], rowB[:], ALU.mult).then_inc(s_r1, 1)

    @blk.tensor
    def _(tensor):
        tensor.wait_ge(s_r1, 3)
        tensor.matmul(pb[4][:], onesf[:], rowB[0:1, 0:512], start=True, stop=True)
        tensor.matmul(pb[5][:], onesf[:], rowB[0:1, 512:1024], start=True, stop=True)
        tensor.matmul(pb[6][:], onesf[:], rowA[0:1, 0:512], start=True, stop=True)
        tensor.matmul(pb[7][:], onesf[:], rowA[0:1, 512:1024], start=True, stop=True).then_inc(s_bm1, 1)

    @blk.vector
    def _(vector):
        vector.wait_ge(s_bm1, 1)
        vector.tensor_copy(rb[:, 0:512], pb[4][:])
        vector.tensor_copy(rb[:, 512:1024], pb[5][:])
        vector.tensor_copy(mrb[:, 0:512], pb[6][:])
        vector.tensor_copy(mrb[:, 512:1024], pb[7][:]).then_inc(s_b1, 1)

    @blk.gpsimd
    def _(pool):
        pool_deq_sb(pool, 0)
        pool_ag_sb(pool, 0)

    @blk.sync
    def _(sync):
        for c in range(NCH):
            if c >= 3:
                xb_load(sync, 2, c, xT, s_sw2, c - 3 + 1)
            else:
                xb_load(sync, 2, c, xT)

    @blk.vector
    def _(vector):
        vector.wait_ge(s_b1, 1)
        for c in range(NCH):
            vector.wait_ge(sxb[c % 3], xb_ready[(2, c)])
            xc = xbuf[c % 3][:]
            vector.tensor_tensor(t1b[:], xc, rb[:], ALU.mult)
            vector.tensor_tensor(h1b[:], t1b[:], mrb[:], ALU.subtract).then_inc(s_sw2, 1)
            for p, (a, b2) in enumerate(PAIRS):
                stt = vector.scalar_tensor_tensor(
                    zd[:], h1b[:, a * 256:(a + 1) * 256], 1.0, h1b[:, b2 * 256:(b2 + 1) * 256],
                    ALU.mult, ALU.mult, accum_out=ssb[:, c * 10 + p:c * 10 + p + 1])
                if c == NCH - 1 and p == len(PAIRS) - 1:
                    stt.then_inc(s_sc, 1)

    @blk.gpsimd
    def _(pool):
        pool.wait_ge(s_sc, 1)
        pool.dma_start(out=sar_i[:, :], in_=ssb[:]).then_inc(s_ar, 16)
        pool.wait_ge(s_ar, 16)
        pool.collective_compute(
            "AllReduce", ALU.add,
            replica_groups=[list(range(NCORE))],
            ins=[sar_i.ap().opt()], outs=[sar_o.ap().opt()],
        ).then_inc(s_arc, 1)

    @blk.vector
    def _(vector):
        for n in range(n_tiles_sb[0], n_tiles_sb[0] + n_tiles_sb[1]):
            deq_tile_compute(vector, n)
        vector.wait_ge(s_ar, 32)
        sf = ssb[:].rearrange("p (c t) -> p c t", t=10)
        s163 = s16[:].rearrange("p (c t) -> p c t", t=16)
        for n in range(4):
            for m in range(4):
                vector.tensor_copy(s163[:, :, n * 4 + m:n * 4 + m + 1], sf[:, :, PIDX[(n, m)]:PIDX[(n, m)] + 1])
        s4 = s16[:].rearrange("p (c n m) -> p c n m", n=4, m=4)
        p4 = p16[:].rearrange("p (c n m) -> p c n m", n=4, m=4)
        vector.tensor_reduce(aux4[:].rearrange("p (c n) -> p c n", n=4), s4, mybir.AxisListType.X, ALU.max)
        mxb = aux4[:].rearrange("p (c n one) -> p c n one", n=4, one=1).broadcast_to([128, NCH, 4, 4])
        vector.tensor_tensor(p4, s4, mxb, ALU.subtract).then_inc(s_sm, 1)

    @blk.scalar
    def _(scalar):
        scalar.wait_ge(s_sm, 1)
        scalar.activation(p16[:], p16[:], AF.Exp, scale=SM_SCALE).then_inc(s_sm, 1)

    @blk.vector
    def _(vector):
        vector.wait_ge(s_sm, 2)
        p4 = p16[:].rearrange("p (c n m) -> p c n m", n=4, m=4)
        vector.tensor_reduce(aux4[:].rearrange("p (c n) -> p c n", n=4), p4, mybir.AxisListType.X, ALU.add)
        vector.reciprocal(aux4[:], aux4[:])
        rcb = aux4[:].rearrange("p (c n one) -> p c n one", n=4, one=1).broadcast_to([128, NCH, 4, 4])
        vector.tensor_tensor(p4, p4, rcb, ALU.mult)

    @blk.sync
    def _(sync):
        def spill(cc):
            slot2 = cc % 2
            sync.wait_ge(s_x1c, cc + 1)
            sync.dma_start(out=x1h[cc * 128:(cc + 1) * 128, :], in_=x1c[slot2][:]).then_inc(ssp2[slot2], 16)
            sp_uses[slot2] += 1
        for c in range(NCH):
            if c >= 3:
                xb_load(sync, 3, c, xT, s_u3, c - 3 + 1)
                spill(c - 3)
            else:
                xb_load(sync, 3, c, xT)
        for cc in range(NCH - 3, NCH):
            spill(cc)

    @blk.vector
    def _(vector):
        for c in range(NCH):
            vector.wait_ge(sxb[c % 3], xb_ready[(3, c)])
            if c >= 2:
                vector.wait_ge(s_c2, c - 2 + 1)
                vector.wait_ge(ssp2[c % 2], 16 * ((c - 2) // 2 + 1))
            xc = xbuf[c % 3][:]
            vector.tensor_tensor(t1b[:], xc, rb[:], ALU.mult)
            vector.tensor_tensor(h1b[:], t1b[:], mrb[:], ALU.subtract)
            for n in range(4):
                dst = macc[:, n * 256:(n + 1) * 256]
                for m in range(4):
                    pav = p16[:, c * 16 + n * 4 + m:c * 16 + n * 4 + m + 1]
                    src = h1b[:, m * 256:(m + 1) * 256]
                    if m == 0:
                        vector.tensor_scalar(dst, src, pav, None, ALU.mult)
                    else:
                        vector.scalar_tensor_tensor(dst, src, pav, dst, ALU.mult, ALU.add)
            vector.tensor_tensor(h1b[:], h1b[:], xc, ALU.add).then_inc(s_u3, 1)
            vector.tensor_tensor(x1c[c % 2][:], h1b[:], macc[:], ALU.add).then_inc(s_x1c, 1)

    @blk.scalar
    def _(scalar):
        for c in range(NCH):
            scalar.wait_ge(s_x1c, c + 1)
            if c >= 2:
                scalar.wait_ge(s_one, 31 + c - 1)
            scalar.activation(xcb[c % 2][:], x1c[c % 2][:], AF.Copy)
            scalar.activation(x2b[c % 2][:], x1c[c % 2][:], AF.Square).then_inc(s_c2, 1)

    @blk.tensor
    def _(tensor):
        tensor.wait_ge(s_b1, 1)
        for c in range(NCH):
            tensor.wait_ge(s_c2, c + 1)
            first = (c == 0); last = (c == NCH - 1)
            mm1 = tensor.matmul(pb[4][0:1, :], ones[:], xcb[c % 2][:, 0:512], start=first, stop=last)
            mm2 = tensor.matmul(pb[5][0:1, :], ones[:], xcb[c % 2][:, 512:1024], start=first, stop=last)
            mm3 = tensor.matmul(pb[6][0:1, :], ones[:], x2b[c % 2][:, 0:512], start=first, stop=last)
            mm4 = tensor.matmul(pb[7][0:1, :], ones[:], x2b[c % 2][:, 512:1024], start=first, stop=last)
            if last:
                for mm in (mm1, mm2, mm3, mm4):
                    mm.then_inc(s_ms2, 1)
            else:
                mm4.then_inc(s_one, 1)

    @blk.vector
    def _(vector):
        vector.wait_ge(s_ms2, 4)
        vector.tensor_copy(rowA[:, 0:512], pb[4][0:1, :])
        vector.tensor_copy(rowA[:, 512:1024], pb[5][0:1, :])
        vector.tensor_copy(rowB[:, 0:512], pb[6][0:1, :])
        vector.tensor_copy(rowB[:, 512:1024], pb[7][0:1, :])
        vector.tensor_scalar(rowA[:], rowA[:], 1.0 / H, None, ALU.mult)
        vector.tensor_scalar(rowB[:], rowB[:], 1.0 / H, None, ALU.mult)
        vector.scalar_tensor_tensor(rb[0:1, :], rowA[:], 1.0, rowA[:], ALU.mult, ALU.mult)
        vector.tensor_tensor(rowB[:], rowB[:], rb[0:1, :], ALU.subtract)
        vector.tensor_scalar(rowB[:], rowB[:], EPS, None, ALU.add).then_inc(s_r2, 1)

    @blk.scalar
    def _(scalar):
        scalar.wait_ge(s_r2, 1)
        scalar.activation(rowB[:], rowB[:], AF.Sqrt).then_inc(s_r2, 1)

    @blk.vector
    def _(vector):
        vector.wait_ge(s_r2, 2)
        vector.reciprocal(rowB[:], rowB[:])
        vector.tensor_tensor(rowA[:], rowA[:], rowB[:], ALU.mult).then_inc(s_r2, 1)

    @blk.tensor
    def _(tensor):
        tensor.wait_ge(s_r2, 3)
        tensor.matmul(pb[4][:], onesf[:], rowB[0:1, 0:512], start=True, stop=True)
        tensor.matmul(pb[5][:], onesf[:], rowB[0:1, 512:1024], start=True, stop=True)
        tensor.matmul(pb[6][:], onesf[:], rowA[0:1, 0:512], start=True, stop=True)
        tensor.matmul(pb[7][:], onesf[:], rowA[0:1, 512:1024], start=True, stop=True).then_inc(s_bm2, 1)

    @blk.vector
    def _(vector):
        vector.wait_ge(s_bm2, 1)
        vector.tensor_copy(rb[:, 0:512], pb[4][:])
        vector.tensor_copy(rb[:, 512:1024], pb[5][:])
        vector.tensor_copy(mrb[:, 0:512], pb[6][:])
        vector.tensor_copy(mrb[:, 512:1024], pb[7][:]).then_inc(s_b2, 1)

    @blk.gpsimd
    def _(pool):
        pool_deq_sb(pool, 1)
        pool_ag_sb(pool, 1)
        pool.wait_ge(s_arc, 1)
        pool.dma_start(out=ssb[:], in_=sar_o[:, :]).then_inc(s_ar, 16)

    @blk.sync
    def _(sync):
        for c in range(NCH):
            sync.wait_ge(ssp2[c % 2], 16 * (c // 2 + 1))
            if c >= 3:
                xb_load(sync, 4, c, x1h, s_xb, c - 3 + 1)
            else:
                xb_load(sync, 4, c, x1h)

    @blk.vector
    def _(vector):
        vector.wait_ge(s_b2, 1)
        for c in range(NCH):
            vector.wait_ge(sxb[c % 3], xb_ready[(4, c)])
            xc = xbuf[c % 3][:]
            vector.tensor_tensor(t1b[:], xc, rb[:], ALU.mult)
            vector.tensor_tensor(xbT[:, c, :], t1b[:], mrb[:], ALU.subtract).then_inc(s_xb, 1)

    at.close()

    ff = ExitStack()
    Tf = lambda *a, **k: ff.enter_context(nc.sbuf_tensor(*a, **k))
    MAXSB = max(SBS)
    hsw = Tf("hsw", [128, MAXSB, 1024], BF16)
    gu = [Tf(f"gu{i}", [128, 2, NCORE, 4, 128], BF16) for i in range(2)]
    wdt = [Tf(f"wdt{i}", [128, MAXSB, 256], BF16) for i in range(2)]
    sg = [Tf(f"sg{i}", [128, 512], F32) for i in range(2)]
    ug = [Tf(f"ug{i}", [128, 512], F32) for i in range(2)]
    evi = [Tf(f"evi{i}", [128, 512], F32) for i in range(3)]
    evo = [Tf(f"evo{i}", [128, 512], F32) for i in range(3)]

    NEV = 64

    @blk.sync
    def _(sync):
        gu_cnt = 0
        wd_cnt = 0
        ei_cnt = 0
        eo_cnt = 0
        for s, sb in enumerate(SBS):
            for j in range(sb):
                slot = gu_cnt % 2
                if gu_cnt >= 2:
                    sync.wait_ge(s_m1, 4 * (gu_cnt - 1))
                sync.wait_ge(sag[s], 3)
                for mat, agsrc in ((0, ag_g_o), (1, ag_u_o)):
                    msrc = agsrc[s].ap().rearrange("(r mb p) im -> p r mb im", p=128, mb=sb)
                    for rh in range(2):
                        sync.dma_start(
                            out=gu[slot][:, mat, rh * 4:(rh + 1) * 4, :, :],
                            in_=msrc[:, rh * 4:(rh + 1) * 4, j, :].rearrange("p r (i m) -> p r i m", i=4)
                        ).then_inc(sgu2[slot], 16)
                gu_uses[slot] += 1
                gu_cnt += 1
            for hbg in range(NCH // 2):
                slot = wd_cnt % 2
                if wd_cnt >= 2:
                    sync.wait_ge(s_m2, 4 * (wd_cnt - 1))
                dsrc = ag_d_o[s].ap().rearrange("(j p) h -> p j h", p=128)
                half = sb // 2
                sync.dma_start(out=wdt[slot][:, 0:half, :],
                               in_=dsrc[:, 0:half, hbg * 256:(hbg + 1) * 256]).then_inc(swd2[slot], 16)
                sync.dma_start(out=wdt[slot][:, half:sb, :],
                               in_=dsrc[:, half:sb, hbg * 256:(hbg + 1) * 256]).then_inc(swd2[slot], 16)
                wd_uses[slot] += 1
                wd_cnt += 1


    @blk.tensor
    def _(tensor):
        tensor.wait_ge(s_xb, NCH)
        gu_cnt = 0
        wd_cnt = 0
        jth = 0
        m2g = 0
        for s, sb in enumerate(SBS):
            for j in range(sb):
                slot = gu_cnt % 2
                gu_cnt += 1
                tensor.wait_ge(sgu2[slot], 64 * ((gu_cnt - 1) // 2 + 1))
                for th in range(2):
                    par = jth % 2
                    if jth >= 2:
                        tensor.wait_ge(s_sg, 2 * (jth - 2) + 2)
                    pg = pb[0 + 2 * par]
                    pu = pb[1 + 2 * par]
                    for hc in range(NCH):
                        mm = tensor.matmul(pg[:], gu[slot][:, 0, hc // 4, hc % 4, :],
                                           xbT[:, hc, th * 512:(th + 1) * 512],
                                           start=(hc == 0), stop=(hc == NCH - 1))
                    mm.then_inc(s_m1, 1)
                    for hc in range(NCH):
                        mm = tensor.matmul(pu[:], gu[slot][:, 1, hc // 4, hc % 4, :],
                                           xbT[:, hc, th * 512:(th + 1) * 512],
                                           start=(hc == 0), stop=(hc == NCH - 1))
                    mm.then_inc(s_m1, 1)
                    jth += 1
            tensor.wait_ge(s_ep, jth)
            for hbg in range(NCH // 2):
                slot = wd_cnt % 2
                wd_cnt += 1
                tensor.wait_ge(swd2[slot], 32 * ((wd_cnt - 1) // 2 + 1))
                for sub in range(2):
                    for th in range(2):
                        po = pb[4 + (m2g % 4)]
                        if m2g >= 4:
                            tensor.wait_ge(s_eo, m2g - 3)
                        for jj in range(sb):
                            mm = tensor.matmul(po[:], wdt[slot][:, jj, sub * 128:(sub + 1) * 128],
                                               hsw[:, jj, th * 512:(th + 1) * 512],
                                               start=(jj == 0), stop=(jj == sb - 1))
                        mm.then_inc(s_m2, 1)
                        m2g += 1

    @blk.scalar
    def _(scalar):
        jth = 0
        ei_cnt = 0
        eo_cnt = 0
        for s, sb in enumerate(SBS):
            def ev_inload(tile):
                nonlocal ei_cnt
                hb2, th2 = tile // 2, tile % 2
                slot2 = (NEV * s + tile) % 3
                if ei_cnt >= 3:
                    scalar.wait_ge(s_eo, NEV * s + tile - 3 + 1)
                if s == 0:
                    scalar.wait_ge(ssp2[hb2 % 2], 16 * (hb2 // 2 + 1))
                    src_ = x1h
                else:
                    scalar.wait_ge(sso4[(NEV * (s - 1) + tile) % 3], so_ready[(s - 1, tile)])
                    src_ = parth
                scalar.dma_start(out=evi[slot2][:],
                                 in_=src_[hb2 * 128:(hb2 + 1) * 128, th2 * 512:(th2 + 1) * 512]).then_inc(sei4[slot2], 16)
                ei_uses[slot2] += 1
                ei_ready[(s, tile)] = 16 * ei_uses[slot2]
                ei_cnt += 1

            def evac_pair(hb):
                nonlocal ei_cnt, eo_cnt
                for th in range(2):
                    tile = hb * 2 + th
                    slot = (NEV * s + tile) % 3
                    if tile + 3 < NEV:
                        ev_inload(tile + 3)
                    scalar.wait_ge(s_eo, NEV * s + tile + 1)
                    dstt = outT if s == NSB - 1 else parth
                    scalar.dma_start(out=dstt[hb * 128:(hb + 1) * 128, th * 512:(th + 1) * 512],
                                     in_=evo[slot][:]).then_inc(sso4[slot], 16)
                    so_uses[slot] += 1
                    so_ready[(s, tile)] = 16 * so_uses[slot]
                    eo_cnt += 1
            for j in range(sb):
                for th in range(2):
                    par = jth % 2
                    if jth >= 2:
                        scalar.wait_ge(s_ep, jth - 1)
                    scalar.wait_ge(s_m1, 2 * jth + 1)
                    scalar.activation(sg[par][:], pb[0 + 2 * par][:], AF.Silu).then_inc(s_sg, 1)
                    scalar.wait_ge(s_m1, 2 * jth + 2)
                    scalar.activation(ug[par][:], pb[1 + 2 * par][:], AF.Copy).then_inc(s_sg, 1)
                    jth += 1
            for t0 in range(3):
                ev_inload(t0)
            for hb in range(NCH):
                evac_pair(hb)
        for jj in range(3):
            scalar.wait_ge(sso4[jj], 16 * so_uses[jj])

    @blk.vector
    def _(vector):
        base2 = n_tiles_sb[0] + n_tiles_sb[1]
        for n in range(base2, base2 + n_tiles_sb[2]):
            deq_tile_compute(vector, n)
        deq_ptr = base2 + n_tiles_sb[2]
        jth = 0
        m2g = 0
        for s, sb in enumerate(SBS):
            for j in range(sb):
                for th in range(2):
                    par = jth % 2
                    vector.wait_ge(s_sg, 2 * jth + 2)
                    vector.tensor_tensor(hsw[:, j, th * 512:(th + 1) * 512], sg[par][:],
                                         ug[par][:], ALU.mult).then_inc(s_ep, 1)
                    jth += 1
                    if jth % 4 != 0 and deq_ptr < NT:
                        deq_tile_compute(vector, deq_ptr)
                        deq_ptr += 1

            for hb in range(NCH):
                for th in range(2):
                    tile = hb * 2 + th
                    g = NEV * s + tile
                    slot = g % 3
                    vector.wait_ge(s_m2, m2g + 1)
                    vector.wait_ge(sei4[slot], ei_ready[(s, tile)])
                    if g >= 3:
                        gp = g - 3
                        vector.wait_ge(sso4[slot], so_ready[(gp // NEV, gp % NEV)])
                    po = pb[4 + (m2g % 4)]
                    vector.tensor_tensor(evo[slot][:], po[:], evi[slot][:], ALU.add).then_inc(s_eo, 1)
                    m2g += 1
        while deq_ptr < NT:
            deq_tile_compute(vector, deq_ptr)
            deq_ptr += 1

    @blk.gpsimd
    def _(pool):
        for s in range(2, NSB):
            pool_deq_sb(pool, s)
            pool_ag_sb(pool, s)

    ff.close()
    st.close()
    return nc


_CACHED = {}


def kernel(L0_attn_x, L0_ffn_gate_q, L0_ffn_gate_absmax, L0_ffn_up_q, L0_ffn_up_absmax,
           L0_ffn_down_q, L0_ffn_down_absmax):
    x = np.asarray(L0_attn_x)
    gq_full = np.ascontiguousarray(np.asarray(L0_ffn_gate_q).astype(np.int16).T)
    uq_full = np.ascontiguousarray(np.asarray(L0_ffn_up_q).astype(np.int16).T)
    dq_full = np.ascontiguousarray(np.asarray(L0_ffn_down_q).astype(np.int16).T)
    gam_full = np.ascontiguousarray(np.asarray(L0_ffn_gate_absmax).reshape(M, 64).T)
    uam_full = np.ascontiguousarray(np.asarray(L0_ffn_up_absmax).reshape(M, 64).T)
    dam_full = np.ascontiguousarray(np.asarray(L0_ffn_down_absmax).reshape(H, M // 64).T)

    if 'nc' not in _CACHED:
        _CACHED['nc'] = build_program()
    nc = _CACHED['nc']

    in_maps = []
    for k in range(NCORE):
        xk = np.ascontiguousarray(x[:, CLOC * k:CLOC * (k + 1), :].reshape(TLOC, H).T)
        drows = []
        damrows = []
        for s, sb in enumerate(SBS):
            for j in range(sb // 8):
                mb = SBOFF[s] + k * (sb // 8) + j
                drows.append(dq_full[mb * 128:(mb + 1) * 128, :])
                damrows.append(dam_full[2 * mb:2 * mb + 2, :])
        in_maps.append({
            "xT": xk,
            "gq": np.ascontiguousarray(gq_full[512 * k:512 * (k + 1), :]),
            "uq": np.ascontiguousarray(uq_full[512 * k:512 * (k + 1), :]),
            "gam": np.ascontiguousarray(gam_full[8 * k:8 * (k + 1), :]),
            "uam": np.ascontiguousarray(uam_full[8 * k:8 * (k + 1), :]),
            "dq": np.ascontiguousarray(np.concatenate(drows, axis=0)),
            "dam": np.ascontiguousarray(np.concatenate(damrows, axis=0)),
        })

    res = run_bass_kernel_spmd(nc, in_maps, core_ids=list(range(NCORE)))
    out = np.empty((B, CSEQ, H), dtype=np.float32)
    for k in range(NCORE):
        ok = np.asarray(res.results[k]["outT"])
        out[:, CLOC * k:CLOC * (k + 1), :] = ok.T.reshape(B, CLOC, H)
    return out
